# version 01 (final)
# baseline (speedup 1.0000x reference)
import numpy as np
import ml_dtypes
import concourse.bass as bass
import concourse.mybir as mybir
from concourse.bass_utils import run_bass_kernel_spmd
from contextlib import ExitStack

F32 = mybir.dt.float32
BF16 = mybir.dt.bfloat16
I32 = mybir.dt.int32
AF = mybir.ActivationFunctionType
ALU = mybir.AluOpType
AX = mybir.AxisListType

ENGINES = ("sync", "tensor", "vector", "scalar", "gpsimd")
DMA_QUEUES = {"sync": "q_sync", "gpsimd": "q_gp", "scalar": "q_sc", "vector": "q_ve"}

EPS = 1e-5
NEG = -1e4

D = 1024
NT = 32
TPB = 4
NB = 8
H = 16
HD = 64

ALL_PHASES = ("p0", "p1", "p2", "p3", "p4", "p5")


def _iv(ap):
    lo = ap.offset
    if not isinstance(lo, int):
        n = 1
        for s in ap.tensor.shape:
            n *= s
        return (ap.tensor.name, 0, n)
    span = 1
    for st, ct in ap.ap:
        if ct > 1:
            span += abs(st) * (ct - 1)
    return (ap.tensor.name, lo, lo + span)


class Node:
    __slots__ = ("i", "eng", "q", "fn", "deps", "seq", "marked")

    def __init__(self, i, eng, q, fn):
        self.i, self.eng, self.q, self.fn = i, eng, q, fn
        self.deps = []
        self.seq = None
        self.marked = False


class TR:

    def __init__(self, nc):
        self.nc = nc
        self.nodes = []
        self.writers = {}
        self.readers = {}
        self.psum_names = set()
        self.scrap = {}

    def note_psum(self, handle):
        self.psum_names.add(handle.name)

    def _intervals(self, aps):
        out = []
        for a in aps:
            name, lo, hi = _iv(a)
            if name in self.psum_names:
                lo, hi = 0, 1 << 60
            out.append((name, lo, hi))
        return out

    def op(self, eng, fn, reads=(), writes=(), queue=None):
        i = len(self.nodes)
        q = queue or eng
        n = Node(i, eng, q, fn)
        rs = self._intervals(reads)
        ws = self._intervals(writes)
        deps = {}

        def add_dep(j):
            nj = self.nodes[j]
            if nj.q == "tensor" and q == "tensor":
                return
            if deps.get(nj.q, -1) < j:
                deps[nj.q] = j

        for name, lo, hi in rs:
            for wlo, whi, j in self.writers.get(name, ()):
                if wlo < hi and lo < whi:
                    add_dep(j)
        for name, lo, hi in ws:
            for wlo, whi, j in self.writers.get(name, ()):
                if wlo < hi and lo < whi:
                    add_dep(j)
            for rlo, rhi, j in self.readers.get(name, ()):
                if rlo < hi and lo < rhi:
                    add_dep(j)
        n.deps = sorted(deps.values())
        if q.startswith("q_"):
            n.marked = True
        for j in n.deps:
            self.nodes[j].marked = True
        for name, lo, hi in rs:
            self.readers.setdefault(name, []).append((lo, hi, i))
        for name, lo, hi in ws:
            wl = self.writers.setdefault(name, [])
            wl[:] = [w for w in wl if not (lo <= w[0] and w[1] <= hi)]
            wl.append((lo, hi, i))
            rl = self.readers.setdefault(name, [])
            rl[:] = [r for r in rl if not (lo <= r[0] and r[1] <= hi)]
        self.nodes.append(n)
        return i

    def barrier(self):
        last = {}
        for n in self.nodes:
            last[n.q] = n.i
        for j in last.values():
            self.nodes[j].marked = True
        deps = sorted(last.values())
        sent = []
        for eng in ENGINES:
            i = len(self.nodes)
            n = Node(i, eng, eng, lambda e: e.nop())
            n.deps = list(deps)
            n.marked = True
            self.nodes.append(n)
            sent.append(i)
        allnames = set(self.writers) | set(self.readers)
        self.writers = {nm: [(0, 1 << 60, s) for s in sent] for nm in allnames}
        self.readers = {nm: [] for nm in allnames}

    def materialize(self, stack):
        nc = self.nc
        sems, counts = {}, {}
        for n in self.nodes:
            if n.marked and n.q not in sems:
                sems[n.q] = stack.enter_context(nc.semaphore("s_" + n.q))
                counts[n.q] = 0
        step = {q: (16 if q.startswith("q_") else 1) for q in sems}
        order = {q: [] for q in sems}
        for n in self.nodes:
            if n.marked:
                counts[n.q] += step[n.q]
                n.seq = counts[n.q]
                order[n.q].append(n.i)
        prog = {e: [] for e in ENGINES}
        know = {e: {} for e in ENGINES}
        ksnap = {}
        COMPUTE_Q = ("vector", "scalar", "gpsimd", "tensor")
        for n in self.nodes:
            K = know[n.eng]
            for j in n.deps:
                d = self.nodes[j]
                assert d.seq is not None
                v = d.seq
                if d.q in COMPUTE_Q and d.q != n.eng:
                    idx = v // step[d.q]
                    lst = order[d.q]
                    if idx < len(lst) and lst[idx] < n.i:
                        v = v + step[d.q]
                if K.get(d.q, 0) < v:
                    prog[n.eng].append(
                        lambda e, s=sems[d.q], vv=v: e.wait_ge(s, vv))
                    K[d.q] = v
                    for qq, kv in ksnap.get(j, {}).items():
                        if K.get(qq, 0) < kv:
                            K[qq] = kv
            ksnap[n.i] = dict(K)
            if n.eng in ("vector", "scalar") and any(
                    self.nodes[j].q == "tensor" for j in n.deps):
                sc = self.scrap.get(n.eng)
                if sc is not None:
                    if n.eng == "vector":
                        prog[n.eng].append(
                            lambda e, s=sc: e.tensor_copy(out=s, in_=s))
                    else:
                        prog[n.eng].append(
                            lambda e, s=sc: e.copy(out=s, in_=s))
            if n.marked:
                prog[n.eng].append(
                    lambda e, fn=n.fn, s=sems[n.q], st=step[n.q]:
                        fn(e).then_inc(s, st))
            else:
                prog[n.eng].append(lambda e, fn=n.fn: fn(e))
        return prog

    def dma(self, eng, out, in_, **kw):
        return self.op(eng, lambda e: e.dma_start(out=out, in_=in_, **kw),
                       reads=(in_,), writes=(out,), queue=DMA_QUEUES[eng])

    def mm(self, out, lhsT, rhs, start=True, stop=True):
        return self.op("tensor",
                       lambda e: e.matmul(out, lhsT, rhs, start=start, stop=stop),
                       reads=(lhsT, rhs) + (() if start else (out,)),
                       writes=(out,))

    def transp(self, out, in_, ident):
        return self.op("tensor", lambda e: e.transpose(out, in_, ident),
                       reads=(in_, ident), writes=(out,))

    def act(self, out, in_, func, bias=0.0, scale=1.0, accum_out=None):
        assert out.shape[0] == in_.shape[0], \
            f"act partition mismatch {out.shape} vs {in_.shape}"
        reads = [in_]
        writes = [out]
        if isinstance(bias, bass.AP):
            reads.append(bias)
        if isinstance(scale, bass.AP):
            reads.append(scale)
        if accum_out is not None:
            writes.append(accum_out)
        if accum_out is not None:
            return self.op("scalar",
                           lambda e: e.activation(out, in_, func, bias=bias,
                                                  scale=scale, accum_out=accum_out),
                           reads=tuple(reads), writes=tuple(writes))
        return self.op("scalar",
                       lambda e: e.activation(out, in_, func, bias=bias, scale=scale),
                       reads=tuple(reads), writes=tuple(writes))

    def tt(self, eng, out, in0, in1, opk):
        return self.op(eng, lambda e: e.tensor_tensor(out, in0, in1, opk),
                       reads=(in0, in1), writes=(out,))

    def ts(self, eng, out, in0, s1, s2, op0, op1=None):
        reads = [in0]
        if isinstance(s1, bass.AP):
            reads.append(s1)
        if isinstance(s2, bass.AP):
            reads.append(s2)
        if op1 is None:
            return self.op(eng, lambda e: e.tensor_scalar(out, in0, s1, s2, op0),
                           reads=tuple(reads), writes=(out,))
        return self.op(eng, lambda e: e.tensor_scalar(out, in0, s1, s2, op0, op1),
                       reads=tuple(reads), writes=(out,))

    def stt(self, eng, out, in0, scal, in1, op0, op1):
        reads = [in0, in1]
        if isinstance(scal, bass.AP):
            reads.append(scal)
        return self.op(eng,
                       lambda e: e.scalar_tensor_tensor(out, in0, scal, in1, op0, op1),
                       reads=tuple(reads), writes=(out,))

    def red(self, eng, out, in_, opk, negate=False):
        return self.op(eng,
                       lambda e: e.tensor_reduce(out, in_, AX.X, opk, negate=negate),
                       reads=(in_,), writes=(out,))

    def copy(self, eng, out, in_):
        if eng == "scalar":
            return self.op(eng, lambda e: e.copy(out=out, in_=in_),
                           reads=(in_,), writes=(out,))
        return self.op(eng, lambda e: e.tensor_copy(out, in_),
                       reads=(in_,), writes=(out,))

    def recip(self, out, in_):
        return self.op("vector", lambda e: e.reciprocal(out, in_),
                       reads=(in_,), writes=(out,))


def build_program(phases=ALL_PHASES):
    nc = bass.Bass("TRN2", target_bir_lowering=False)
    ap = {}

    def din(name, shape, dt=F32):
        ap[name] = nc.dram_tensor(name, shape, dt, kind="ExternalInput").ap()

    def dout(name, shape, dt=F32):
        ap[name] = nc.dram_tensor(name, shape, dt, kind="ExternalOutput").ap()

    def dint(name, shape, dt=F32):
        ap[name] = nc.dram_tensor(name, shape, dt, kind="Internal").ap()

    din("xc", [4096, D])
    din("x0full", [NB * 4096, D])
    din("xgp", [64, D])
    din("Wq", [D, D]); din("Wk", [D, D]); din("Wv", [D, D]); din("Wo", [D, D])
    din("WkT", [D, D])
    din("bq", [D]); din("bk", [D]); din("bv", [D]); din("bo", [D])
    din("bqf", [128, 8]); din("bkf", [128, 8]); din("bvf", [128, 8])
    din("Wg0", [D, 8]); din("bg0", [8]); din("We", [D, D]); din("be", [D])
    din("wqkvT", [D, 3 * D]); din("woutT", [D, D])
    din("lnw", [D]); din("lnb", [D])
    din("Wg1", [D, 8]); din("bg1", [8])
    din("W1", [8, D, D]); din("b1", [8, D]); din("W2", [8, D, D]); din("b2", [8, D])
    din("ident", [128, 128], BF16)
    din("ones_row", [1, 128], BF16)
    din("maskB0", [128, 128], BF16)
    din("mask1", [128, 128], BF16)
    din("dupmask", [4, 128, 128])
    din("rowmaskT", [128, 4])
    din("ehot", [1, 4 * 128], BF16)
    din("cid", [1, 16], I32)
    dout("yc", [4096, D])
    dout("dbg1", [128, 8 * D]); dout("dbg2", [64, 8 * D])
    dout("dbg3", [64, 8 * D]); dout("dbg4", [128, 8 * D])
    dout("dbg5", [128, 8 * D]); dout("dbg6", [128, 8 * D])
    dout("dbg7", [128, 64]); dout("dbg8", [128, 8 * D])
    dint("gp_dram", [64, D], BF16)
    dint("qbk_dram", [64, 16])
    dint("wqkv_bf", [D, 3 * D], BF16)
    dint("wo_bf", [D, D], BF16)
    dint("vg_dram", [64, D], BF16)
    dint("wout_bf", [D, D], BF16)
    dint("rowflat_dram", [1, NB * D], BF16)

    with ExitStack() as stack:
        def mk_sb(st):
            return lambda name, shape, dt: st.enter_context(
                nc.sbuf_tensor(name, shape, dt))

        def mk_ps(st):
            def f(name, shape, dt=F32):
                h = st.enter_context(nc.psum_tensor(name, shape, dt))
                tr.note_psum(h)
                return h
            return f

        tr = TR(nc)
        sb = mk_sb(stack)
        X = sb("X", [128, NT * D], BF16)
        ident = sb("ident_s", [128, 128], BF16)
        ones_row = sb("ones_s", [1, 128], BF16)
        tr.dma("sync", ident[:, :], ap["ident"])
        tr.dma("sync", ones_row[:, :], ap["ones_row"])
        stg = [sb(f"stg{i}", [128, D], F32) for i in range(2)]
        stg.append(stg[1])
        epsc = sb("epsc", [128, 1], F32)
        tr.op("vector", lambda e: e.memset(epsc[:, :], EPS),
              reads=(), writes=(epsc[:, :],))
        onec = sb("onec", [128, 1], BF16)
        tr.op("vector", lambda e: e.memset(onec[:, :], 1.0),
              reads=(), writes=(onec[:, :],))
        sqscr = sb("sqscr", [128, D], BF16)
        scrapv = sb("scrapv", [1, 8], F32)
        scraps = sb("scraps", [1, 8], F32)
        tr.op("vector", lambda e: e.memset(scrapv[:, :], 0.0),
              reads=(), writes=(scrapv[:, :],))
        tr.op("vector", lambda e: e.memset(scraps[:, :], 0.0),
              reads=(), writes=(scraps[:, :],))
        tr.scrap["vector"] = scrapv[0:1, 0:2]
        tr.scrap["scalar"] = scraps[0:1, 0:2]

        _alt = [0]

        def alt():
            _alt[0] ^= 1
            return "vector" if _alt[0] else "scalar"

        def ln_stats(xt, sc, scratch):
            np_ = xt.shape[0]
            tr.red("vector", sc[:, 2:3], xt, ALU.add)
            tr.act(scratch[0:np_, :], xt, AF.Square, accum_out=sc[:, 3:4])
            tr.ts("vector", sc[:, 0:1], sc[:, 2:3], 1.0 / D, None, ALU.mult)
            tr.ts("vector", sc[:, 2:3], sc[:, 3:4], 1.0 / D, None, ALU.mult)
            tr.stt("vector", sc[:, 3:4], sc[:, 0:1], sc[:, 0:1], sc[:, 2:3],
                   ALU.mult, ALU.subtract)
            tr.act(sc[:, 2:3], sc[:, 3:4], AF.Sqrt, bias=epsc[0:np_, 0:1],
                   scale=-1.0)
            tr.recip(sc[:, 1:2], sc[:, 2:3])

        def ln_apply(out, xt, sc):
            tr.ts("vector", out, xt, sc[:, 0:1], sc[:, 1:2], ALU.subtract, ALU.mult)

        _tp_flip = [0]

        def transpose_8(dst_ap, src, pTpair):
            _tp_flip[0] ^= 1
            pT = pTpair[_tp_flip[0]]
            for k in range(8):
                tr.transp(pT[:, 128 * k:128 * (k + 1)],
                          src[:, 128 * k:128 * (k + 1)], ident[:, :])
            tr.copy(alt(), dst_ap, pT[:, :])

        def load_w(sbf, dstname, src_ap, nchunk=8):
            dst = sbf(dstname, [128, nchunk * D], BF16)
            for fc in range(nchunk):
                s = stg[2]
                tr.dma("sync", s[:, :], src_ap[128 * fc:128 * (fc + 1), :])
                tr.copy(alt(), dst[:, D * fc:D * (fc + 1)], s[:, :])
            return dst

        def bias_row(sbf, name, src_ap, n=D):
            t = sbf(name, [1, n], BF16)
            s = stg[2]
            tr.dma("sync", s[0:1, 0:n], src_ap)
            tr.copy(alt(), t[0:1, :], s[0:1, 0:n])
            return t

        if "p0" in phases:
            for t in range(NT):
                s = stg[t % 2]
                tr.dma("sync", s[:, :], ap["xc"][128 * t:128 * (t + 1), :])
                tr.copy(alt(), X[:, D * t:D * (t + 1)], s[:, :])

        if ("p1" in phases) or ("p2" in phases):
            with ExitStack() as phL0:
                sb0 = mk_sb(phL0)
                ps0 = mk_ps(phL0)
                Wqb = load_w(sb0, "Wqb", ap["Wq"])
                Wkb = load_w(sb0, "Wkb", ap["Wk"])
                Wvb = load_w(sb0, "Wvb", ap["Wv"])
                bq_fm = sb0("bq_fm", [128, 8], F32)
                bk_fm = sb0("bk_fm", [128, 8], F32)
                bv_fm = sb0("bv_fm", [128, 8], F32)
                tr.dma("sync", bq_fm[:, :], ap["bqf"])
                tr.dma("sync", bk_fm[:, :], ap["bkf"])
                tr.dma("sync", bv_fm[:, :], ap["bvf"])
                bv_row = bias_row(sb0, "bv_row", ap["bv"][None, :])
                bo_row = bias_row(sb0, "bo_row", ap["bo"][None, :])

                qgp = sb0("qgp", [64, 16 * 64], BF16)
                kgp = sb0("kgp", [64, 16 * 64], BF16)
                vgb = sb0("vgb", [8, D], BF16)
                gp_ps_scope = phL0.enter_context(ExitStack())
                ps0g = mk_ps(gp_ps_scope)
                sb0g = mk_sb(gp_ps_scope)
                xgp_bf = sb0g("xgp_bf", [64, D], BF16)
                xgp_sc = sb0g("xgp_sc", [64, 8], F32)
                s = stg[2]
                tr.dma("sync", s[0:64, :], ap["xgp"])
                tr.copy("vector", xgp_bf[:, :], s[0:64, :])
                ln_stats(xgp_bf[:, :], xgp_sc, sqscr[:, :])
                ln_apply(xgp_bf[:, :], xgp_bf[:, :], xgp_sc)
                pT0 = ps0g("pT0", [128, 512], BF16)
                xgpT = sb0g("xgpT", [128, 8 * 64], BF16)
                for fc in range(8):
                    tr.transp(pT0[:, 64 * fc:64 * (fc + 1)],
                              xgp_bf[:, 128 * fc:128 * (fc + 1)], ident[0:64, 0:64])
                tr.copy("vector", xgpT[:, :], pT0[:, :])

                def proj_gp_fm(dst, dstname, w, bias_fm, scale):
                    pQ = ps0g("p_" + dstname, [128, 64], F32)
                    for oc in range(8):
                        for fc in range(8):
                            tr.mm(pQ[:, :],
                                  w[:, D * fc + 128 * oc:D * fc + 128 * (oc + 1)],
                                  xgpT[:, 64 * fc:64 * (fc + 1)],
                                  start=(fc == 0), stop=(fc == 7))
                        tr.ts("vector", dst[0:64, 64 * 2 * oc:64 * (2 * oc + 1)],
                              pQ[0:64, :], bias_fm[0:64, oc:oc + 1], scale,
                              ALU.add, ALU.mult)
                        tr.ts("vector",
                              dst[0:64, 64 * (2 * oc + 1):64 * (2 * oc + 2)],
                              pQ[64:128, :], bias_fm[64:128, oc:oc + 1], scale,
                              ALU.add, ALU.mult)

                proj_gp_fm(qgp, "qgp", Wqb, bq_fm, 0.125)
                proj_gp_fm(kgp, "kgp", Wkb, bk_fm, 1.0)
                pVg = ps0g("pVg", [8, 512], F32)
                for b in range(NB):
                    for half in range(2):
                        for fc in range(8):
                            tr.mm(pVg[:, :],
                                  xgpT[:, 64 * fc + 8 * b:64 * fc + 8 * b + 8],
                                  Wvb[:, D * fc + 512 * half:D * fc + 512 * (half + 1)],
                                  start=(fc == 0), stop=False)
                        tr.mm(pVg[:, :], ones_row[0:1, 0:8],
                              bv_row[0:1, 512 * half:512 * (half + 1)],
                              start=False, stop=True)
                        tr.copy(alt(), vgb[0:8, 512 * half:512 * (half + 1)],
                                pVg[0:8, :])
                    tr.dma("gpsimd", ap["vg_dram"][8 * b:8 * (b + 1), :],
                           vgb[0:8, :])

                tr.barrier()
                gp_ps_scope.close()
                rowflat = None
                if "p1" in phases:
                    with ExitStack() as ph1:
                        sb1 = mk_sb(ph1)
                        ps1 = mk_ps(ph1)
                        C_sb = sb1("C_sb", [128, 8 * 8 * 128], BF16)
                        bv_fold = sb1("bv_fold", [64, 16], F32)
                        qbkrow = sb1("qbkrow", [1, NB * 128], BF16)
                        csc_row = sb1("csc_row", [1, NB * 128], BF16)
                        xb_t = [sb1(f"p1xb{i}", [128, D], BF16) for i in range(2)]
                        xT_t = [sb1(f"p1xT{i}", [128, D], BF16) for i in range(2)]
                        sc_t = [sb1(f"p1sc{i}", [128, 8], F32) for i in range(2)]
                        aug_t = [sb1(f"p1aug{i}", [128, 4], BF16) for i in range(2)]
                        mrow_t = [sb1(f"p1mr{i}", [1, 128], BF16) for i in range(2)]
                        et_t = [sb1(f"p1et{i}", [128, 128], BF16) for i in range(2)]
                        E_sb = sb1("E_sb", [128, D], BF16)
                        E_T = sb1("E_T", [128, D], BF16)
                        escal = sb1("escal", [128, 2], F32)
                        den_sb = sb1("den_sb", [128, NB], F32)
                        agp_fm = sb1("agp_fm", [64, 128], BF16)
                        rden_sb = sb1("rden_sb", [128, 1], BF16)
                        rden_f = sb1("rden_f", [128, 1], F32)
                        rdrow = sb1("rdrow", [1, 128], BF16)
                        bcast_sb = sb1("bcast_sb", [64, 128], F32)
                        atb = sb1("atb", [8, D], BF16)
                        cid_sb = sb1("cid_sb", [1, 16], I32)
                        p1a_scope = ph1.enter_context(ExitStack())
                        ps1a = mk_ps(p1a_scope)
                        sb1a = mk_sb(p1a_scope)
                        WkTf = [sb1a(f"WkTf{i}", [64, D], BF16) for i in range(2)]
                        Ctmp = sb1a("Ctmp", [128, 8 * 8 * 128], BF16)
                        pC = ps1a("pC1", [128, 64])
                        for h in range(H):
                            wkh = WkTf[h % 2]
                            s = stg[h % 2]
                            tr.dma("sync", s[0:64, :],
                                   ap["WkT"][64 * h:64 * (h + 1), :])
                            tr.copy(alt(), wkh[0:64, :], s[0:64, :])
                            for fc in range(8):
                                tr.mm(pC[:, :],
                                      wkh[0:64, 128 * fc:128 * (fc + 1)],
                                      qgp[0:64, 64 * h:64 * (h + 1)],
                                      start=True, stop=True)
                                tr.copy("vector",
                                        Ctmp[:, (h * 8 + fc) * 64:
                                             (h * 8 + fc) * 64 + 64],
                                        pC[:, :])
                        for fc in range(8):
                            for b in range(NB):
                                srcC = bass.AP(Ctmp, fc * 64 + b * 8,
                                               [[Ctmp.shape[1], 128],
                                                [1, 8], [512, 16]])
                                tr.copy("vector",
                                        C_sb[:, fc * 1024 + b * 128:
                                             fc * 1024 + (b + 1) * 128],
                                        srcC)
                        pqb = ps1a("pqb", [64, 16])
                        btmp = sb1a("btmp", [64, 32], F32)
                        tr.copy("vector", btmp[0:64, 0:8], bk_fm[0:64, :])
                        tr.copy("vector", btmp[0:64, 8:16], bk_fm[64:128, :])
                        tr.copy("vector", btmp[0:64, 16:24], bv_fm[0:64, :])
                        tr.copy("vector", btmp[0:64, 24:32], bv_fm[64:128, :])
                        bk_fold = sb1a("bk_fold", [64, 16], BF16)
                        tr.copy("vector", bk_fold[0:64, :],
                                bass.AP(btmp, 0, [[btmp.shape[1], 64],
                                                  [1, 8], [8, 2]]))
                        tr.copy("vector", bv_fold[0:64, :],
                                bass.AP(btmp, 16, [[btmp.shape[1], 64],
                                                   [1, 8], [8, 2]]))
                        for h in range(H):
                            tr.mm(pqb[:, h:h + 1],
                                  qgp[0:64, 64 * h:64 * (h + 1)],
                                  bk_fold[0:64, h:h + 1],
                                  start=True, stop=True)
                        qbk_sb = sb1a("qbk_sb", [64, 16], F32)
                        tr.copy("vector", qbk_sb[:, :], pqb[:, :])
                        tr.dma("gpsimd", ap["qbk_dram"], qbk_sb[:, :])
                        tr.dma("gpsimd", stg[0][0:1, :],
                               ap["qbk_dram"].rearrange("a b -> (a b)")[None, :])
                        tr.copy("vector", qbkrow[:, :], stg[0][0:1, :])
                        pcs = ps1a("pcs", [128, 8])
                        for b in range(NB):
                            for fc in range(8):
                                tr.mm(pcs[:, b:b + 1],
                                      C_sb[:, fc * 1024 + b * 128:
                                           fc * 1024 + (b + 1) * 128],
                                      onec[:, :], start=(fc == 0), stop=(fc == 7))
                        csc_col = sb1a("csc_col", [128, 8], BF16)
                        tr.ts("vector", csc_col[:, :], pcs[:, :], -1.0, None,
                              ALU.mult)
                        pRow = ps1a("pRow", [1, 128], BF16)
                        for b in range(NB):
                            tr.transp(pRow[:, :], csc_col[:, b:b + 1], ident[:, :])
                            tr.copy(alt(), csc_row[0:1, 128 * b:128 * (b + 1)],
                                    pRow[0:1, :])
                        tr.barrier()
                        p1a_scope.close()
                        pE = ps1("pE", [128, 1536])
                        pS1 = ps1("pS1", [128, 128])
                        pT1 = [ps1(f"pT1{i}", [128, D], BF16) for i in range(2)]
                        pMu = ps1("pMu", [1, 128], BF16)
                        pB1 = ps1("pB1", [64, 128])
                        pA1 = pS1
                        pR1 = pMu
                        pAT = pT1[0]
                        for b in range(NB):
                            for k in range(NT):
                                i2 = k % 2
                                xb, xT, sc = xb_t[i2], xT_t[i2], sc_t[i2]
                                aug, mrow, et = aug_t[i2], mrow_t[i2], et_t[i2]
                                s = stg[k % 2]
                                row0 = b * 4096 + 128 * k
                                tr.dma("sync", s[:, :],
                                       ap["x0full"][row0:row0 + 128, :])
                                tr.copy("vector", xb[:, :], s[:, :])
                                ln_stats(xb[:, :], sc, sqscr[:, :])
                                tr.copy("vector", aug[:, 0:1], sc[:, 0:1])
                                tr.copy("vector", aug[:, 1:2], sc[:, 2:3])
                                tr.act(sc[:, 4:5], sc[:, 1:2], AF.Ln)
                                tr.ts("vector", aug[:, 2:3], sc[:, 0:1], -1.0,
                                      None, ALU.mult)
                                tr.transp(pMu[:, :], aug[:, 2:3], ident[:, :])
                                tr.copy("scalar", mrow[0:1, :], pMu[0:1, :])
                                transpose_8(xT[:, :], xb[:, :], pT1)
                                for fc in range(8):
                                    tr.mm(pS1[:, :], xT[:, 128 * fc:128 * (fc + 1)],
                                          C_sb[:, fc * 1024 + b * 128:
                                               fc * 1024 + (b + 1) * 128],
                                          start=(fc == 0), stop=False)
                                tr.mm(pS1[:, :], mrow[0:1, :],
                                      csc_row[0:1, 128 * b:128 * (b + 1)],
                                      start=False, stop=False)
                                tr.mm(pS1[:, :], ones_row[0:1, :],
                                      qbkrow[0:1, 128 * b:128 * (b + 1)],
                                      start=False, stop=True)
                                tr.act(et[:, :], pS1[:, :], AF.Exp,
                                       bias=sc[:, 4:5], scale=sc[:, 1:2])
                                tr.mm(pE[:, 0:512], et[:, :], xb[:, 0:512],
                                      start=(k == 0), stop=(k == NT - 1))
                                tr.mm(pE[:, 512:1024], et[:, :], xb[:, 512:1024],
                                      start=(k == 0), stop=(k == NT - 1))
                                tr.mm(pE[:, 1024:1026], et[:, :], aug[:, 0:2],
                                      start=(k == 0), stop=(k == NT - 1))
                            tr.copy("scalar", escal[:, :], pE[:, 1024:1026])
                            tr.ts("vector", E_sb[:, :], pE[:, 0:1024],
                                  escal[:, 0:1], None, ALU.subtract)
                            tr.copy("vector", den_sb[:, b:b + 1], escal[:, 1:2])
                            transpose_8(E_T[:, :], E_sb[:, :], pT1)
                            for h in range(H):
                                rhsap = bass.AP(E_T, 128 * (h // 2) * 0, [[0, 0]])
                                for fc in range(8):
                                    rh = bass.AP(E_T, 128 * fc + h,
                                                 [[E_T.shape[1], 128],
                                                  [16, 8]])
                                    tr.op("tensor",
                                          lambda e, o=pA1, lt=Wvb, rr=rh, hh=h,
                                          ff=fc, st=(fc == 0), sp=(fc == 7):
                                          e.matmul(
                                              o[64 * (hh % 2):64 * (hh % 2) + 64,
                                                8 * (hh // 2):8 * (hh // 2) + 8],
                                              lt[:, D * ff + 64 * hh:
                                                 D * ff + 64 * (hh + 1)],
                                              rr, start=st, stop=sp),
                                          reads=(Wvb[:, :], rh) +
                                          (() if fc == 0 else (pA1[:, :],)),
                                          writes=(pA1[:, :],))
                            tr.recip(rden_f[:, :], den_sb[:, b:b + 1])
                            tr.copy("vector", rden_sb[:, :], rden_f[:, :])
                            tr.transp(pR1[:, :], rden_sb[:, :], ident[:, :])
                            tr.copy("vector", rdrow[0:1, :], pR1[0:1, :])
                            srcr = bass.AP(rdrow, 0,
                                           [[rdrow.shape[1], 1],
                                            [1, 16], [16, 8]])
                            tr.op("tensor",
                                  lambda e, o=pB1, sr=srcr: e.matmul(
                                      o[0:64, :], ones_row[0:1, 0:64], sr,
                                      start=True, stop=True),
                                  reads=(ones_row[0:1, :], srcr),
                                  writes=(pB1[:, :],))
                            tr.copy("scalar", bcast_sb[:, :], pB1[0:64, :])
                            tr.tt("vector", agp_fm[:, :], pA1[0:64, :],
                                  bcast_sb[:, :], ALU.mult)
                            tr.tt("vector", agp_fm[:, :], agp_fm[:, :],
                                  bv_fold[:, :, None].broadcast_to([64, 16, 8]),
                                  ALU.add)
                            for h in range(H):
                                tr.transp(pAT[0:8, 64 * h:64 * (h + 1)],
                                          agp_fm[0:64, 8 * h:8 * (h + 1)],
                                          ident[0:64, 0:64])
                            tr.copy("vector", atb[:, :], pAT[0:8, :])
                            tr.dma("gpsimd", ap["gp_dram"][8 * b:8 * (b + 1), :],
                                   atb[:, :])
                        tr.dma("sync", cid_sb[:, :], ap["cid"])
                        rcid = stack.enter_context(nc.gpsimd.register("rcid"))

                        def rf0(e):
                            e.reg_load(rcid, cid_sb[0:1, 0:1])
                            return e.dma_start(
                                out=ap["rowflat_dram"][0:1, 0:D],
                                in_=ap["gp_dram"][
                                    bass.ds(bass.RuntimeValue(rcid), 1), :])
                        tr.op("gpsimd", rf0,
                              reads=(ap["gp_dram"], cid_sb[:, :]),
                              writes=(ap["rowflat_dram"],), queue="q_gp")
                        for b2 in range(1, NB):
                            tr.op("gpsimd",
                                  lambda e, bb=b2: e.dma_start(
                                      out=ap["rowflat_dram"][0:1,
                                                             D * bb:D * (bb + 1)],
                                      in_=ap["gp_dram"][
                                          bass.ds(bass.RuntimeValue(rcid) + 8 * bb,
                                                  1), :]),
                                  reads=(ap["gp_dram"], cid_sb[:, :]),
                                  writes=(ap["rowflat_dram"],), queue="q_gp")
                        rowflat = True
                        tr.barrier()

                if "p2" in phases:
                    with ExitStack() as ph2:
                        sb2 = mk_sb(ph2)
                        ps2 = mk_ps(ph2)
                        wbuf2 = [sb2(f"p2wo{i}", [128, D], BF16) for i in range(2)]
                        vgb2 = sb2("vgb2", [8, D], BF16)
                        rowflat_b = sb2("rowflat_b", [1, D], BF16)
                        for fc in range(8):
                            s = stg[fc % 2]
                            tr.dma("sync", s[:, :], ap["Wo"][128 * fc:128 * (fc + 1), :])
                            tr.copy(alt(), wbuf2[fc % 2][:, :], s[:, :])
                            tr.dma("gpsimd", ap["wo_bf"][128 * fc:128 * (fc + 1), :],
                                   wbuf2[fc % 2][:, :])
                        dup_sb = sb2("dup_sb", [128, 4 * 128], BF16)
                        for qt in range(4):
                            s = stg[qt % 2]
                            tr.dma("sync", s[:, 0:128], ap["dupmask"][qt])
                            tr.copy(alt(), dup_sb[:, 128 * qt:128 * (qt + 1)],
                                    s[:, 0:128])
                        maskB0_sb = sb2("maskB0_sb", [128, 128], BF16)
                        tr.dma("sync", maskB0_sb[:, :], ap["maskB0"])
                        rm_sb = sb2("rm_sb", [128, 4], F32)
                        tr.dma("sync", rm_sb[:, :], ap["rowmaskT"])
                        ehot_sb = sb2("ehot_sb", [1, 4 * 128], BF16)
                        tr.dma("sync", ehot_sb[0:1, :], ap["ehot"])
                        xlnT_b = sb2("xlnT_b", [128, 8 * 512], BF16)
                        xln_s = [sb2(f"p2xln{i}", [128, D], BF16) for i in range(2)]
                        sc2_t = [sb2(f"p2sc{i}", [128, 8], F32) for i in range(2)]
                        Qf = sb2("Qf", [64, 16 * 512], BF16)
                        Kf = sb2("Kf", [64, 16 * 512], BF16)
                        Vt = sb2("Vt", [128, D], BF16)
                        eg_sb = sb2("eg_sb", [128, 128], BF16)
                        eTg_sb = sb2("eTg_sb", [8, 1024], BF16)
                        e_sb = sb2("e_sb", [128, 1024], BF16)
                        eT_sb = sb2("eT_sb", [128, 1024], BF16)
                        den2 = sb2("den2", [128, 16], F32)
                        deng = sb2("deng", [128, 16], F32)
                        rden2 = sb2("rden2", [128, 16], F32)
                        attn_tm = sb2("attn_tm", [128, 1024], BF16)
                        attnT = sb2("attnT", [128, 1024], BF16)
                        pS_A = ps2("pS_A", [128, 1024])
                        pS_B = ps2("pS_B", [128, 1024])
                        pPV = ps2("pPV", [128, 1024])
                        pM = [ps2(f"pM{i}", [128, 1024], BF16) for i in range(2)]
                        pTg = pM[0][0:8, :]
                        for b in range(NB):
                            tr.dma("sync", vgb2[0:8, :],
                                   ap["vg_dram"][8 * b:8 * (b + 1), :])
                            if rowflat is not None:
                                tr.dma("sync", rowflat_b[0:1, :],
                                       ap["rowflat_dram"][0:1, D * b:D * (b + 1)])
                            for pair in range(2):
                                for tt2 in range(2):
                                    tt4 = 2 * pair + tt2
                                    t = 4 * b + tt4
                                    xts = X[:, D * t:D * (t + 1)]
                                    ln_stats(xts, sc2_t[tt2], sqscr[:, :])
                                    ln_apply(xln_s[tt2][:, :], xts, sc2_t[tt2])
                                for cc in range(8):
                                    pMx = pM[cc % 2]
                                    for tt2 in range(2):
                                        tr.transp(pMx[:, 128 * tt2:128 * (tt2 + 1)],
                                                  xln_s[tt2][:, 128 * cc:
                                                             128 * (cc + 1)],
                                                  ident[:, :])
                                    tr.copy(alt(),
                                            xlnT_b[:, 512 * cc + 256 * pair:
                                                   512 * cc + 256 * (pair + 1)],
                                            pMx[:, 0:256])
                            for oc in range(8):
                                for qk in range(2):
                                    w, dst, bias = ((Wqb, Qf, bq_fm)
                                                    if qk == 0 else
                                                    (Wkb, Kf, bk_fm))
                                    pp = pS_A if qk == 0 else pS_B
                                    for fc in range(8):
                                        tr.mm(pp[:, 0:512],
                                              w[:, D * fc + 128 * oc:
                                                D * fc + 128 * (oc + 1)],
                                              xlnT_b[:, 512 * fc:512 * (fc + 1)],
                                              start=(fc == 0), stop=(fc == 7))
                                    tr.ts("vector",
                                          dst[0:64, 512 * 2 * oc:512 * (2 * oc + 1)],
                                          pp[0:64, 0:512], bias[0:64, oc:oc + 1],
                                          None, ALU.add)
                                    tr.ts("vector",
                                          dst[0:64, 512 * (2 * oc + 1):
                                              512 * (2 * oc + 2)],
                                          pp[64:128, 0:512],
                                          bias[64:128, oc:oc + 1], None, ALU.add)
                            for qt in range(4):
                                for half in range(2):
                                    for fc in range(8):
                                        tr.mm(pPV[:, 512 * half:512 * (half + 1)],
                                              xlnT_b[:, 512 * fc + 128 * qt:
                                                     512 * fc + 128 * (qt + 1)],
                                              Wvb[:, D * fc + 512 * half:
                                                  D * fc + 512 * (half + 1)],
                                              start=(fc == 0), stop=False)
                                    tr.mm(pPV[:, 512 * half:512 * (half + 1)],
                                          ones_row[0:1, :],
                                          bv_row[0:1, 512 * half:512 * (half + 1)],
                                          start=False, stop=True)
                                tr.copy(alt(), Vt[:, :], pPV[:, :])
                                for h in range(H):
                                    tr.mm(pS_A[:, 8 * h:8 * (h + 1)],
                                          Qf[0:64, 512 * h + 128 * qt:
                                             512 * h + 128 * (qt + 1)],
                                          kgp[0:64, 64 * h + 8 * b:
                                              64 * h + 8 * (b + 1)],
                                          start=True, stop=True)
                                tr.tt("vector", pS_A[:, 0:128], pS_A[:, 0:128],
                                      dup_sb[:, 128 * qt:128 * (qt + 1)], ALU.add)
                                tr.act(eg_sb[:, :], pS_A[:, 0:128], AF.Exp,
                                       scale=0.125)
                                tr.red("vector", deng[:, :],
                                       eg_sb[:, :].rearrange("p (h g) -> p h g",
                                                             h=16), ALU.add)
                                for grp in range(2):
                                    for hh in range(8):
                                        h = 8 * grp + hh
                                        ql = Qf[0:64, 512 * h + 128 * qt:
                                                512 * h + 128 * (qt + 1)]
                                        kl = Kf[0:64, 512 * h + 128 * qt:
                                                512 * h + 128 * (qt + 1)]
                                        tr.mm(pS_A[:, 128 * hh:128 * (hh + 1)],
                                              ql, kl, start=True, stop=True)
                                        tr.mm(pS_B[:, 128 * hh:128 * (hh + 1)],
                                              kl, ql, start=True, stop=True)
                                    tr.act(e_sb[:, :], pS_A[:, :], AF.Exp,
                                           scale=0.125)
                                    tr.act(eT_sb[:, :], pS_B[:, :], AF.Exp,
                                           scale=0.125)
                                    tr.tt("vector", e_sb[:, :], e_sb[:, :],
                                          maskB0_sb[:, None, :].broadcast_to(
                                              [128, 8, 128]), ALU.mult)
                                    tr.tt("vector", eT_sb[:, :], eT_sb[:, :],
                                          maskB0_sb[:, None, :].broadcast_to(
                                              [128, 8, 128]), ALU.mult)
                                    tr.red("vector", den2[:, 8 * grp:8 * (grp + 1)],
                                           e_sb[:, :].rearrange(
                                               "p (h k) -> p h k", h=8), ALU.add)
                                    pTgx = pM[grp % 2][0:8, :]
                                    for hh in range(8):
                                        h = 8 * grp + hh
                                        tr.transp(pTgx[:, 128 * hh:128 * (hh + 1)],
                                                  eg_sb[:, 8 * h:8 * (h + 1)],
                                                  ident[0:128, :]) if False else \
                                        tr.transp(pTgx[:, 128 * hh:128 * (hh + 1)],
                                                  eg_sb[:, 8 * h:8 * (h + 1)],
                                                  ident[:, :])
                                    tr.copy(alt(), eTg_sb[:, :], pTgx[:, :])
                                    for hh in range(8):
                                        h = 8 * grp + hh
                                        tr.mm(pPV[:, 64 * h:64 * (h + 1)],
                                              eT_sb[:, 128 * hh:128 * (hh + 1)],
                                              Vt[:, 64 * h:64 * (h + 1)],
                                              start=True, stop=False)
                                        tr.mm(pPV[:, 64 * h:64 * (h + 1)],
                                              eTg_sb[0:8, 128 * hh:
                                                     128 * (hh + 1)],
                                              vgb2[0:8, 64 * h:64 * (h + 1)],
                                              start=False, stop=True)
                                tr.tt("vector", den2[:, :], den2[:, :],
                                      deng[:, :], ALU.add)
                                tr.recip(rden2[:, :], den2[:, :])
                                tr.tt("vector", attn_tm[:, :], pPV[:, :],
                                      rden2[:, :, None].broadcast_to(
                                          [128, 16, 64]), ALU.mult)
                                if rowflat is not None:
                                    for half in range(2):
                                        tr.mm(pS_A[:, 512 * half:512 * (half + 1)],
                                              ehot_sb[0:1, 128 * qt:128 * (qt + 1)],
                                              rowflat_b[0:1, 512 * half:
                                                        512 * (half + 1)],
                                              start=True, stop=True)
                                        tr.stt("vector",
                                               attn_tm[:, 512 * half:512 * (half + 1)],
                                               attn_tm[:, 512 * half:512 * (half + 1)],
                                               rm_sb[:, qt:qt + 1],
                                               pS_A[:, 512 * half:512 * (half + 1)],
                                               ALU.mult, ALU.add)
                                pMx = pM[qt % 2]
                                for k in range(8):
                                    tr.transp(pMx[:, 128 * k:128 * (k + 1)],
                                              attn_tm[:, 128 * k:128 * (k + 1)],
                                              ident[:, :])
                                tr.copy(alt(), attnT[:, :], pMx[:, :])
                                t = 4 * b + qt
                                for fc in range(8):
                                    tr.dma("sync", wbuf2[fc % 2][:, :],
                                           ap["wo_bf"][128 * fc:128 * (fc + 1), :])
                                    for half in range(2):
                                        tr.mm(pS_B[:, 512 * half:512 * (half + 1)],
                                              attnT[:, 128 * fc:128 * (fc + 1)],
                                              wbuf2[fc % 2][:, 512 * half:
                                                            512 * (half + 1)],
                                              start=(fc == 0), stop=False)
                                for half in range(2):
                                    tr.mm(pS_B[:, 512 * half:512 * (half + 1)],
                                          ones_row[0:1, :],
                                          bo_row[0:1, 512 * half:512 * (half + 1)],
                                          start=False, stop=True)
                                    xo = X[:, D * t + 512 * half:
                                           D * t + 512 * (half + 1)]
                                    tr.tt("vector", xo,
                                          pS_B[:, 512 * half:512 * (half + 1)],
                                          xo, ALU.add)
                        tr.barrier()
                tr.barrier()

        if "p3" in phases:
            with ExitStack() as ph3:
                sb3 = mk_sb(ph3)
                ps3 = mk_ps(ph3)
                Web = load_w(sb3, "Web", ap["We"])
                Wgb = sb3("Wgb", [128, 64], BF16)
                s = stg[2]
                tr.dma("sync", s[:, 0:64].rearrange("p (c e) -> p c e", c=8),
                       ap["Wg0"].rearrange("(c p) e -> p c e", p=128))
                tr.copy("vector", Wgb[:, :], s[:, 0:64])
                bg_row = bias_row(sb3, "bg0_row", ap["bg0"][None, :], 8)
                be_row = bias_row(sb3, "be_row", ap["be"][None, :])
                xln = [sb3(f"p3xln{i}", [128, D], BF16) for i in range(2)]
                xlnT = [sb3(f"p3xlnT{i}", [128, D], BF16) for i in range(2)]
                sc_t = [sb3(f"p3sc{i}", [128, 8], F32) for i in range(2)]
                gsc = [sb3(f"p3gsc{i}", [128, 12], F32) for i in range(2)]
                scr = sb3("p3scr", [128, D], BF16)
                pT = [ps3(f"p3T{i}", [128, D], BF16) for i in range(2)]
                pG = ps3("p3G", [128, 8])
                pO = [ps3(f"p3O{i}", [128, 512]) for i in range(2)]
                for t in range(NT):
                    i2 = t % 2
                    xt = X[:, D * t:D * (t + 1)]
                    ln_stats(xt, sc_t[i2], sqscr[:, :])
                    ln_apply(xln[i2][:, :], xt, sc_t[i2])
                    transpose_8(xlnT[i2][:, :], xln[i2][:, :], pT)
                    for fc in range(8):
                        tr.mm(pG[:, :], xlnT[i2][:, 128 * fc:128 * (fc + 1)],
                              Wgb[:, 8 * fc:8 * (fc + 1)],
                              start=(fc == 0), stop=False)
                    tr.mm(pG[:, :], ones_row[0:1, :], bg_row[0:1, :],
                          start=False, stop=True)
                    g = gsc[i2]
                    tr.red("vector", g[:, 0:1], pG[:, :], ALU.max, negate=True)
                    tr.act(g[:, 4:12], pG[:, :], AF.Exp, bias=g[:, 0:1])
                    tr.red("vector", g[:, 1:2], g[:, 4:12], ALU.add)
                    tr.recip(g[:, 2:3], g[:, 1:2])
                    for half in range(2):
                        for fc in range(8):
                            tr.mm(pO[half][:, :],
                                  xlnT[i2][:, 128 * fc:128 * (fc + 1)],
                                  Web[:, D * fc + 512 * half:
                                      D * fc + 512 * (half + 1)],
                                  start=(fc == 0), stop=False)
                        tr.mm(pO[half][:, :], ones_row[0:1, :],
                              be_row[0:1, 512 * half:512 * (half + 1)],
                              start=False, stop=True)
                        xo = X[:, D * t + 512 * half:D * t + 512 * (half + 1)]
                        tr.stt("vector", xo, pO[half][:, :], g[:, 2:3], xo,
                               ALU.mult, ALU.add)
                tr.barrier()

        import os as _os
        P4LIM = int(_os.environ.get("P4LIM", "9"))
        P4SUB = int(_os.environ.get("P4SUB", "9"))
        if "p4" in phases:
            with ExitStack() as ph4:
                sb4 = mk_sb(ph4)
                ps4 = mk_ps(ph4)
                wrow = sb4("wrow", [128, 3 * D], BF16)
                for rc in range(8):
                    for cc in range(3):
                        s = stg[cc % 2]
                        tr.dma("sync", s[:, :],
                               ap["wqkvT"][128 * rc:128 * (rc + 1),
                                           D * cc:D * (cc + 1)])
                        tr.copy(alt(), wrow[:, D * cc:D * (cc + 1)], s[:, :])
                    tr.dma("gpsimd", ap["wqkv_bf"][128 * rc:128 * (rc + 1), :],
                           wrow[:, :])
                wstg4 = sb4("wstg4", [128, D], BF16)
                for fc in range(8):
                    s = stg[fc % 2]
                    tr.dma("sync", s[:, :], ap["woutT"][128 * fc:128 * (fc + 1), :])
                    tr.copy(alt(), wstg4[:, :], s[:, :])
                    tr.dma("gpsimd", ap["wout_bf"][128 * fc:128 * (fc + 1), :],
                           wstg4[:, :])
                lnw_rep = sb4("lnw_rep", [128, D], BF16)
                lnb_rep = sb4("lnb_rep", [128, D], BF16)
                tr.dma("sync", stg[0][:, :],
                       ap["lnw"][None, :].broadcast_to([128, D]))
                tr.dma("sync", stg[1][:, :],
                       ap["lnb"][None, :].broadcast_to([128, D]))
                tr.copy("vector", lnw_rep[:, :], stg[0][:, :])
                tr.copy("vector", lnb_rep[:, :], stg[1][:, :])
                tr.copy("vector", lnw_rep[:, :], stg[0][:, :])
                tr.copy("vector", lnb_rep[:, :], stg[1][:, :])
                if _os.environ.get("P4DBG"):
                    tr.dma("gpsimd", ap["dbg4"][:, 2 * D:3 * D],
                           lnb_rep[:, :])
                    tr.dma("gpsimd", ap["dbg4"][:, 3 * D:4 * D],
                           lnw_rep[:, :])
                mask1_sb = sb4("mask1_sb", [128, 128], BF16)
                tr.dma("sync", mask1_sb[:, :], ap["mask1"])
                Qpi = sb4("Qpi", [64, 8 * D], BF16)
                Kpi = sb4("Kpi", [64, 8 * D], BF16)
                Vt_g = sb4("Vt_g", [128, 512], BF16)
                attn_tm4 = sb4("attn_tm4", [128, 8 * D], BF16)
                sc4 = [sb4(f"p4sc{i}", [128, 8], F32) for i in range(2)]
                sc4b = [sb4(f"p4scb{i}", [128, 8], F32) for i in range(2)]
                xin2 = [sb4(f"p4xin{i}", [128, D], BF16) for i in range(NB)]
                den_w = sb4("den_w", [128, 128], F32)
                rden_w = sb4("rden_w", [128, 128], F32)
                e4 = sb4("e4", [128, 1024], BF16)
                eT4 = sb4("eT4", [128, 1024], BF16)
                xln4 = [sb4(f"p4xln{i}", [128, D], BF16) for i in range(2)]
                pT4 = [ps4(f"pT4{i}", [128, D], BF16) for i in range(2)]
                pSs = ps4("pSs", [128, 1024])
                pSt = ps4("pSt", [128, 1024])
                pPV4 = ps4("pPV4", [128, 512])
                for w in range(4):
                    with ExitStack() as phw:
                        sbw = mk_sb(phw)
                        inner = phw.enter_context(ExitStack())
                        sbi = mk_sb(inner)
                        xlnT_pi = sbi(f"xlnT_pi{w}", [128, 8 * D], BF16)
                        vbuf = sbi(f"vbuf{w}", [128, 8 * D], BF16)
                        wbuf = [sbi(f"wbuf{w}_{i}", [128, D], BF16)
                                for i in range(2)]
                        for b in range(NB):
                            i2 = b % 2
                            t = 4 * b + w
                            xts = X[:, D * t:D * (t + 1)]
                            ln_stats(xts, sc4[i2], sqscr[:, :])
                            ln_apply(xln4[i2][:, :], xts, sc4[i2])
                            tr.tt("vector", xts, xts, xln4[i2][:, :], ALU.add)
                            ln_stats(xln4[i2][:, :], sc4b[i2], sqscr[:, :])
                            ln_apply(xin2[b][:, :], xln4[i2][:, :], sc4b[i2])
                            tr.tt("vector", xin2[b][:, :], xin2[b][:, :],
                                  lnw_rep[:, :], ALU.mult)
                            tr.tt("vector", xin2[b][:, :], xin2[b][:, :],
                                  lnb_rep[:, :], ALU.add)
                            if w == 0 and _os.environ.get("P4DBG"):
                                tr.dma("gpsimd", ap["dbg7"][:, 8 * b:8 * b + 4],
                                       sc4b[i2][:, 0:4])
                                tr.dma("gpsimd", ap["dbg7"][:, 8 * b + 4:
                                                            8 * b + 8],
                                       sc4[i2][:, 0:4])
                        for cc in range(8):
                            pTx = pT4[cc % 2]
                            for b in range(NB):
                                tr.transp(pTx[:, 128 * b:128 * (b + 1)],
                                          xin2[b][:, 128 * cc:128 * (cc + 1)],
                                          ident[:, :])
                            srcI = bass.AP(pTx, 0, [[pTx.shape[1], 128],
                                                    [1, 128], [128, 8]])
                            tr.copy(alt(), xlnT_pi[:, D * cc:D * (cc + 1)],
                                    srcI)
                        if w == 0 and _os.environ.get("P4DBG"):
                            tr.dma("gpsimd", ap["dbg1"], xlnT_pi[:, :])
                            for b5 in range(NB):
                                tr.dma("gpsimd",
                                       ap["dbg5"][:, D * b5:D * (b5 + 1)],
                                       xin2[b5][:, :])
                        for fc in range(8):
                            vsrc = bass.AP(ap["wqkv_bf"].tensor,
                                           3 * D * 128 * fc + 128,
                                           [[3 * D, 128], [192, 16], [1, 64]])
                            tr.dma("sync",
                                   vbuf[:, D * fc:D * (fc + 1)].rearrange(
                                       "p (n j) -> p n j", n=16), vsrc)
                        for hhalf in range(2 if P4LIM >= 2 else 0):
                            for qk in range(2):
                                dstpi = Qpi if qk == 0 else Kpi
                                for oc2 in range(4):
                                    oc = 4 * hhalf + oc2
                                    wb = wbuf[oc2 % 2]
                                    for fc in range(8):
                                        wsrc = bass.AP(
                                            ap["wqkv_bf"].tensor,
                                            3 * D * 128 * fc + 384 * oc + 64 * qk,
                                            [[3 * D, 128], [192, 2], [1, 64]])
                                        tr.dma("sync",
                                               wb[:, 128 * fc:128 * (fc + 1)]
                                               .rearrange("p (hh j) -> p hh j",
                                                          hh=2), wsrc)
                                    for half in range(2):
                                        for fc in range(8):
                                            tr.mm(pSs[:, 512 * half:
                                                      512 * (half + 1)],
                                                  wb[:, 128 * fc:128 * (fc + 1)],
                                                  xlnT_pi[:, D * fc + 512 * half:
                                                          D * fc + 512 * (half + 1)],
                                                  start=(fc == 0), stop=(fc == 7))
                                    hh0 = 2 * oc2
                                    tr.copy(alt(),
                                            dstpi[0:64, D * hh0:D * (hh0 + 1)],
                                            pSs[0:64, :])
                                    tr.copy(alt(),
                                            dstpi[0:64, D * (hh0 + 1):
                                                  D * (hh0 + 2)],
                                            pSs[64:128, :])
                            if w == 0 and hhalf == 0 and _os.environ.get("P4DBG"):
                                tr.dma("gpsimd", ap["dbg2"], Qpi[0:64, :])
                                tr.dma("gpsimd", ap["dbg3"], Kpi[0:64, :])
                            if P4LIM < 3:
                                continue
                            for g8 in range(8):
                                for fc in range(8):
                                    tr.mm(pPV4[:, :],
                                          xlnT_pi[:, D * fc + 128 * g8:
                                                  D * fc + 128 * (g8 + 1)],
                                          vbuf[:, D * fc + 512 * hhalf:
                                               D * fc + 512 * (hhalf + 1)],
                                          start=(fc == 0), stop=(fc == 7))
                                tr.copy(alt(), Vt_g[:, :], pPV4[:, :])
                                if P4SUB < 2:
                                    continue
                                for hh in range(8):
                                    ql = Qpi[0:64, D * hh + 128 * g8:
                                             D * hh + 128 * (g8 + 1)]
                                    kl = Kpi[0:64, D * hh + 128 * g8:
                                             D * hh + 128 * (g8 + 1)]
                                    tr.mm(pSs[:, 128 * hh:128 * (hh + 1)],
                                          ql, kl, start=True, stop=True)
                                    tr.mm(pSt[:, 128 * hh:128 * (hh + 1)],
                                          kl, ql, start=True, stop=True)
                                if P4SUB >= 3:
                                    tr.act(e4[:, :], pSs[:, :], AF.Exp,
                                           scale=0.03125)
                                    tr.act(eT4[:, :], pSt[:, :], AF.Exp,
                                           scale=0.03125)
                                if P4SUB < 4:
                                    continue
                                tr.tt("vector", e4[:, :], e4[:, :],
                                      mask1_sb[:, None, :].broadcast_to(
                                          [128, 8, 128]), ALU.mult)
                                tr.tt("vector", eT4[:, :], eT4[:, :],
                                      mask1_sb[:, None, :].broadcast_to(
                                          [128, 8, 128]), ALU.mult)
                                tr.red("vector",
                                       den_w[:, 16 * g8 + 8 * hhalf:
                                             16 * g8 + 8 * (hhalf + 1)],
                                       e4[:, :].rearrange("p (h k) -> p h k",
                                                          h=8), ALU.add)
                                for hh in range(8):
                                    tr.mm(pPV4[:, 64 * hh:64 * (hh + 1)],
                                          eT4[:, 128 * hh:128 * (hh + 1)],
                                          Vt_g[:, 64 * hh:64 * (hh + 1)],
                                          start=True, stop=True)
                                tr.copy(alt(),
                                        attn_tm4[:, D * g8 + 512 * hhalf:
                                                 D * g8 + 512 * (hhalf + 1)],
                                        pPV4[:, :])
                        if P4LIM >= 3 and P4SUB >= 5:
                            tr.recip(rden_w[:, :], den_w[:, :])
                            for g8 in range(8):
                                tr.tt("vector",
                                      attn_tm4[:, D * g8:D * (g8 + 1)],
                                      attn_tm4[:, D * g8:D * (g8 + 1)],
                                      rden_w[:, 16 * g8:16 * (g8 + 1), None]
                                      .broadcast_to([128, 16, 64]), ALU.mult)
                        if w == 0 and _os.environ.get("P4DBG"):
                            tr.dma("gpsimd", ap["dbg4"][:, 0:D], lnw_rep[:, :])
                            tr.dma("gpsimd", ap["dbg4"][:, D:2 * D],
                                   lnb_rep[:, :])
                            tr.dma("gpsimd", ap["dbg6"], attn_tm4[:, :])
                        tr.barrier()
                        inner.close()
                        attnT_w = sbw(f"attnT_w{w}", [128, 8 * D], BF16)
                        wbuf3 = [sbw(f"wb3_{w}_{i}", [128, D], BF16)
                                 for i in range(2)]
                        for g8 in range(8 if P4LIM >= 4 else 0):
                            pTx = pT4[g8 % 2]
                            for k in range(8):
                                tr.transp(pTx[:, 128 * k:128 * (k + 1)],
                                          attn_tm4[:, D * g8 + 128 * k:
                                                   D * g8 + 128 * (k + 1)],
                                          ident[:, :])
                            for cc in range(8):
                                tr.copy(alt(),
                                        attnT_w[:, D * cc + 128 * g8:
                                               D * cc + 128 * (g8 + 1)],
                                        pTx[:, 128 * cc:128 * (cc + 1)])
                        for b in range(NB if P4LIM >= 4 else 0):
                            t = 4 * b + w
                            for fc in range(8):
                                wb4 = wbuf3[fc % 2]
                                tr.dma("sync", wb4[:, :],
                                       ap["wout_bf"][128 * fc:128 * (fc + 1), :])
                                for half in range(2):
                                    lh = bass.AP(attnT_w, D * fc + b,
                                                 [[attnT_w.shape[1], 128],
                                                  [8, 128]])
                                    tr.op("tensor",
                                          lambda e, o=pSs, hf=half, ll=lh, ff=fc,
                                          ww=wb4: e.matmul(
                                              o[:, 512 * hf:512 * (hf + 1)], ll,
                                              ww[:, 512 * hf:512 * (hf + 1)],
                                              start=(ff == 0), stop=(ff == 7)),
                                          reads=(lh, wb4[:, :]) +
                                          (() if fc == 0 else (pSs[:, :],)),
                                          writes=(pSs[:, :],))
                            for half in range(2):
                                xo = X[:, D * t + 512 * half:
                                       D * t + 512 * (half + 1)]
                                tr.tt("vector", xo,
                                      pSs[:, 512 * half:512 * (half + 1)], xo,
                                      ALU.add)
                        if w == 0 and _os.environ.get("P4DBG"):
                            for b5 in range(NB):
                                tr.dma("gpsimd",
                                       ap["dbg8"][:, D * b5:D * (b5 + 1)],
                                       X[:, D * (4 * b5):D * (4 * b5 + 1)])
                        tr.barrier()
                tr.barrier()

        if "p5" in phases:
            with ExitStack() as ph5:
                sb5 = mk_sb(ph5)
                ps5 = mk_ps(ph5)
                xlnT_full = sb5("xlnT_full", [128, NT * D], BF16)
                ind5 = sb5("ind5", [128, NT * 8], F32)
                Wg1b = sb5("Wg1b", [128, 64], BF16)
                s = stg[2]
                tr.dma("sync", s[:, 0:64].rearrange("p (c e) -> p c e", c=8),
                       ap["Wg1"].rearrange("(c p) e -> p c e", p=128))
                tr.copy("vector", Wg1b[:, :], s[:, 0:64])
                bg1_row = bias_row(sb5, "bg1_row", ap["bg1"][None, :], 8)
                b1_nat = sb5("b1_nat", [8, D], BF16)
                s = stg[2]
                tr.dma("sync", s[0:8, :], ap["b1"])
                tr.copy("vector", b1_nat[:, :], s[0:8, :])
                pB5 = ps5("pB5", [128, 64], BF16)
                for cc in range(8):
                    tr.transp(pB5[:, 8 * cc:8 * (cc + 1)],
                              b1_nat[0:8, 128 * cc:128 * (cc + 1)],
                              ident[0:8, 0:8])
                b1_sb = sb5("b1_sb", [128, 64], F32)
                tr.copy("vector", b1_sb[:, :], pB5[:, :])
                b2_row = sb5("b2_row", [1, 8 * D], BF16)
                for ee in range(8):
                    s = stg[ee % 2]
                    tr.dma("sync", s[0:1, :], ap["b2"][ee:ee + 1, :])
                    tr.copy(alt(), b2_row[0:1, D * ee:D * (ee + 1)], s[0:1, :])
                xln5 = [sb5(f"p5xln{i}", [128, D], BF16) for i in range(4)]
                sc5 = [sb5(f"p5sc{i}", [128, 8], F32) for i in range(2)]

                pT5 = [ps5(f"pT5{i}", [128, D], BF16) for i in range(2)]
                pG5 = ps5("pG5", [128, 8])
                mx5 = sb5("mx5", [128, 2], F32)
                for w8 in range(8):
                    for tt in range(4):
                        t = 4 * w8 + tt
                        xt = X[:, D * t:D * (t + 1)]
                        ln_stats(xt, sc5[tt % 2], sqscr[:, :])
                        ln_apply(xln5[tt][:, :], xt, sc5[tt % 2])
                    for cc in range(8):
                        pTx = pT5[cc % 2]
                        for tt in range(4):
                            tr.transp(pTx[:, 128 * tt:128 * (tt + 1)],
                                      xln5[tt][:, 128 * cc:128 * (cc + 1)],
                                      ident[:, :])
                        tr.copy(alt(),
                                xlnT_full[:, 4096 * w8 + 512 * cc:
                                          4096 * w8 + 512 * (cc + 1)],
                                pTx[:, 0:512])
                    for tt in range(4):
                        t = 4 * w8 + tt
                        for fc in range(8):
                            tr.mm(pG5[:, :],
                                  xlnT_full[:, 4096 * w8 + 512 * fc + 128 * tt:
                                            4096 * w8 + 512 * fc + 128 * (tt + 1)],
                                  Wg1b[:, 8 * fc:8 * (fc + 1)],
                                  start=(fc == 0), stop=False)
                        tr.mm(pG5[:, :], ones_row[0:1, :], bg1_row[0:1, :],
                              start=False, stop=True)
                        tr.red("vector", mx5[:, 0:1], pG5[:, :], ALU.max)
                        tr.ts("vector", ind5[:, 8 * t:8 * (t + 1)], pG5[:, :],
                              mx5[:, 0:1], None, ALU.is_ge)
                W1b = sb5("W1b", [128, 8 * D], BF16)
                W2b = sb5("W2b", [128, 8 * D], BF16)
                h_sb = sb5("h_sb", [128, 8 * 512], BF16)
                pH5 = ps5("pH5", [128, 512])
                pY5 = [ps5(f"pY5{i}", [128, 512]) for i in range(2)]
                for ee in range(8):
                    for fc in range(8):
                        s = stg[fc % 2]
                        tr.dma("sync", s[:, :],
                               ap["W1"][ee, 128 * fc:128 * (fc + 1), :])
                        tr.copy(alt(), W1b[:, D * fc:D * (fc + 1)], s[:, :])
                    for fc in range(8):
                        s = stg[fc % 2]
                        tr.dma("sync", s[:, :],
                               ap["W2"][ee, 128 * fc:128 * (fc + 1), :])
                        tr.copy(alt(), W2b[:, D * fc:D * (fc + 1)], s[:, :])
                    for w8 in range(8):
                        for oc in range(8):
                            for fc in range(8):
                                tr.mm(pH5[:, :],
                                      W1b[:, D * fc + 128 * oc:
                                          D * fc + 128 * (oc + 1)],
                                      xlnT_full[:, 4096 * w8 + 512 * fc:
                                                4096 * w8 + 512 * (fc + 1)],
                                      start=(fc == 0), stop=(fc == 7))
                            tr.act(h_sb[:, 512 * oc:512 * (oc + 1)], pH5[:, :],
                                   AF.Gelu, bias=b1_sb[:, 8 * oc + ee:
                                                       8 * oc + ee + 1])
                        for tt4 in range(4):
                            t = 4 * w8 + tt4
                            for half in range(2):
                                for ic in range(8):
                                    tr.mm(pY5[half][:, :],
                                          h_sb[:, 512 * ic + 128 * tt4:
                                               512 * ic + 128 * (tt4 + 1)],
                                          W2b[:, D * ic + 512 * half:
                                              D * ic + 512 * (half + 1)],
                                          start=(ic == 0), stop=False)
                                tr.mm(pY5[half][:, :], ones_row[0:1, :],
                                      b2_row[0:1, D * ee + 512 * half:
                                             D * ee + 512 * (half + 1)],
                                      start=False, stop=True)
                                xo = X[:, D * t + 512 * half:
                                       D * t + 512 * (half + 1)]
                                tr.stt("vector", xo, pY5[half][:, :],
                                       ind5[:, 8 * t + ee:8 * t + ee + 1], xo,
                                       ALU.mult, ALU.add)
                tr.barrier()

        for t in range(NT):
            s = stg[t % 2]
            tr.copy(alt(), s[:, :], X[:, D * t:D * (t + 1)])
            tr.dma("gpsimd", ap["yc"][128 * t:128 * (t + 1), :], s[:, :])
        tr.op("gpsimd", lambda e: e.nop(), reads=(ap["yc"],), writes=())

        prog = tr.materialize(stack)
        with nc.Block() as block:
            for eng in ENGINES:
                ops = prog[eng]
                getattr(block, eng)(lambda e, ops=ops: [f(e) for f in ops])
    return nc, ap


_GP = [585 * c for c in range(8)]


def make_inputs(inputs, core):
    bf = ml_dtypes.bfloat16
    x0 = np.asarray(inputs["L0_attn_x"], np.float32)
    c = core
    m = {}
    m["xc"] = np.ascontiguousarray(
        x0[:, 512 * c:512 * (c + 1), :].reshape(4096, D))
    m["x0full"] = x0.reshape(NB * 4096, D)
    m["xgp"] = np.ascontiguousarray(x0[:, _GP, :].reshape(64, D))
    m["Wq"] = inputs["L0_attn_Wq"]; m["Wk"] = inputs["L0_attn_Wk"]
    m["Wv"] = inputs["L0_attn_Wv"]; m["Wo"] = inputs["L0_attn_Wo"]
    m["WkT"] = np.ascontiguousarray(np.asarray(inputs["L0_attn_Wk"]).T)
    m["bq"] = inputs["L0_attn_bq"]; m["bk"] = inputs["L0_attn_bk"]
    m["bv"] = inputs["L0_attn_bv"]; m["bo"] = inputs["L0_attn_bo"]
    for nm, src_ in (("bqf", "L0_attn_bq"), ("bkf", "L0_attn_bk"),
                     ("bvf", "L0_attn_bv")):
        m[nm] = np.ascontiguousarray(
            np.asarray(inputs[src_], np.float32).reshape(8, 128).T)
    m["Wg0"] = inputs["L0_ffn_Wg"]; m["bg0"] = inputs["L0_ffn_bg"]
    m["We"] = inputs["L0_ffn_We"]; m["be"] = inputs["L0_ffn_be"]
    m["wqkvT"] = np.ascontiguousarray(np.asarray(inputs["L1_attn_w_qkv"]).T)
    m["woutT"] = np.ascontiguousarray(np.asarray(inputs["L1_attn_w_out"]).T)
    m["lnw"] = inputs["L1_attn_ln_w"]; m["lnb"] = inputs["L1_attn_ln_b"]
    m["Wg1"] = inputs["L1_ffn_Wg"]; m["bg1"] = inputs["L1_ffn_bg"]
    m["W1"] = inputs["L1_ffn_W1"]; m["b1"] = inputs["L1_ffn_b1"]
    m["W2"] = inputs["L1_ffn_W2"]; m["b2"] = inputs["L1_ffn_b2"]
    m["ident"] = np.eye(128, dtype=bf)
    m["ones_row"] = np.ones((1, 128), dtype=bf)
    b0 = np.zeros((128, 128), np.float32)
    b0[:64, :64] = 1.0
    b0[64:, 64:] = 1.0
    m["maskB0"] = b0.astype(bf)
    m1 = np.zeros((128, 128), np.float32)
    for g in range(16):
        m1[8 * g:8 * (g + 1), 8 * g:8 * (g + 1)] = 1.0
    m["mask1"] = m1.astype(bf)
    dup = np.zeros((4, 128, 128), np.float32)
    rm = np.ones((4, 128), np.float32)
    eh = np.zeros((4, 128), np.float32)
    qt = (73 * c) // 128
    r = (73 * c) % 128
    blk_half = ((73 * c) // 64) % 2
    dup[qt, 64 * blk_half:64 * (blk_half + 1), [8 * h + c for h in range(16)]] = NEG
    rm[qt, r] = 0.0
    eh[qt, r] = 1.0
    m["dupmask"] = dup
    m["rowmaskT"] = np.ascontiguousarray(rm.T)
    m["ehot"] = eh.reshape(1, 512).astype(bf)
    m["cid"] = np.full((1, 16), c, np.int32)
    return {k2: np.ascontiguousarray(np.asarray(v)) for k2, v in m.items()}


_NC_CACHE = {}


def kernel(**inputs):
    phases = ALL_PHASES
    key = tuple(phases)
    if key not in _NC_CACHE:
        _NC_CACHE[key] = build_program(phases)
    nc, _ = _NC_CACHE[key]
    in_maps = [make_inputs(inputs, c) for c in range(8)]
    res = run_bass_kernel_spmd(nc, in_maps, core_ids=list(range(8)))
    out = np.empty((NB, 4096, D), np.float32)
    for c in range(8):
        out[:, 512 * c:512 * (c + 1), :] = res.results[c]["yc"].reshape(NB, 512, D)
    return out
